# version 10
# baseline (speedup 1.0000x reference)
import math
import numpy as np
import concourse.bass as bass
import concourse.mybir as mybir
from concourse.bass_utils import run_bass_kernel_spmd

F32 = mybir.dt.float32
BF16 = mybir.dt.bfloat16
I32 = mybir.dt.int32
AF = mybir.ActivationFunctionType
ALU = mybir.AluOpType
AX = mybir.AxisListType

P = 128
D = 1024
H = 16
DH = 64
SEQ = 2048
BL = 2
TLOC = BL * SEQ
NT = TLOC // P
NTS = SEQ // P
E = 8
CAP = 4096
FF = 4096
KC = 2176
KOFF = 128
NKCH = KC // P
T_BAND = 60.0
GELU_S = float((2.0 ** 0.5) / 1.41421)
NEG = -1.0e30

_slopes_np = None
def slopes_list():
    global _slopes_np
    if _slopes_np is None:
        start = 2.0 ** (-2.0 ** (-(math.log2(H) - 3.0)))
        _slopes_np = np.array([start * start ** i for i in range(H)], dtype=np.float32)
    return [float(x) for x in _slopes_np]

def qtw_of(h):
    s = slopes_list()[h]
    for w in (512, 256, 128):
        if s * (w / 2.0) <= 45.3:
            return w
    return 128

def chunk_list(h, i0, qtw):
    band = T_BAND / slopes_list()[h]
    c_hi = i0 + qtw - 1 + KOFF
    c_lo = i0 - band + KOFF
    lo = max(0, int(math.floor(c_lo / P)) * P)
    hi = min(KC - P, int(math.floor(c_hi / P)) * P)
    return list(range(lo, hi + 1, P))

DBG = False
DBG_OUT = False
PHASES = 4
P0CUT = 99

class KB:
    def __init__(self, nc, es):
        self.nc = nc
        self.es = es
        self.ops = {"sync": [], "scalar": [], "vector": [], "tensor": [], "gpsimd": []}
        self.semn = {}
        self.sems = {}

    def sem(self, name):
        h = self.es.enter_context(self.nc.semaphore(name))
        self.sems[name] = h
        self.semn[name] = 0

    def emit(self, eng, fn):
        self.ops[eng].append(fn)

    def wait(self, eng, sname, val=None):
        v = self.semn[sname] if val is None else val
        if v <= 0:
            return
        h = self.sems[sname]
        self.emit(eng, lambda e, h=h, v=v: e.wait_ge(h, v))

    def n(self, sname):
        return self.semn[sname]

    def dma(self, q, out, in_, sname):
        h = self.sems[sname]
        self.semn[sname] += 16
        self.emit(q, lambda e, o=out, i=in_, h=h: e.dma_start(out=o, in_=i).then_inc(h, 16))
        return self.semn[sname]

    def idma(self, out, out_off, in_, in_off, sname):
        h = self.sems[sname]
        self.semn[sname] += 16
        self.emit("gpsimd", lambda e, o=out, oo=out_off, i=in_, io=in_off, h=h:
                  e.indirect_dma_start(out=o, out_offset=oo, in_=i, in_offset=io).then_inc(h, 16))
        return self.semn[sname]

    def op(self, eng, fn, sname=None, k=1):
        if eng == "vector":
            if sname is None:
                sname = "eV"
            h = self.sems[sname]
            self.semn[sname] += k
            self.emit(eng, lambda e, f=fn, h=h, k=k: f(e).then_inc(h, k))
            v = self.semn[sname]
            self.wait(eng, sname, v)
            return v
        if sname is None:
            self.emit(eng, fn)
            return None
        h = self.sems[sname]
        self.semn[sname] += k
        self.emit(eng, lambda e, f=fn, h=h, k=k: f(e).then_inc(h, k))
        return self.semn[sname]

    def op_sync(self, eng, fn, sname="eV"):
        v = self.op(eng, fn, sname)
        self.wait(eng, sname, v)
        return v

    def build(self):
        ops = self.ops
        with self.nc.Block() as block:
            @block.sync
            def _(sync):
                for f in ops["sync"]:
                    f(sync)
            @block.scalar
            def _(scalar):
                for f in ops["scalar"]:
                    f(scalar)
            @block.vector
            def _(vector):
                for f in ops["vector"]:
                    f(vector)
            @block.tensor
            def _(tensor):
                for f in ops["tensor"]:
                    f(tensor)
            @block.gpsimd
            def _(gpsimd):
                for f in ops["gpsimd"]:
                    f(gpsimd)


def build_program():
    from contextlib import ExitStack
    es = ExitStack()
    nc = bass.Bass("TRN2", target_bir_lowering=False, detect_race_conditions=False)
    es.enter_context(nc.allow_non_contiguous_dma(reason="small one-time strided loads"))
    es.enter_context(nc.allow_low_precision(reason="bf16 psum transposes are exact data movement"))
    kb = KB(nc, es)
    slopes = slopes_list()

    dram = {}
    def din(name, shape, dt=F32):
        dram[name] = nc.dram_tensor(name, shape, dt, kind="ExternalInput").ap()
        return dram[name]
    x_in = din("x", [BL, SEQ, D])
    c_in = din("cvec", [BL, D])
    wmod_in = din("wmod", [D, 6 * D])
    gamma_in = din("gamma", [D])
    wq_in = din("wq", [D, D])
    wkv_in = din("wkv", [D, 2 * D])
    nullkv_in = din("nullkv", [H, 4, DH])
    qs_in = din("qs", [DH])
    ks_in = din("ks", [DH])
    wout_in = din("wout", [D, D])
    wsw_in = din("wswd", [D, E])
    bsw_in = din("bsw", [E])
    w1_in = din("w1e", [D, FF])
    b1_in = din("b1e", [FF])
    w2_in = din("w2e", [FF, D])
    b2_in = din("b2e", [D])
    cmask_in = din("cmask", [1, 64])
    pmask_in = din("pmask", [1, 64])
    out_d = nc.dram_tensor("out", [BL, SEQ, D], F32, kind="ExternalOutput").ap()
    DK = "ExternalOutput" if (DBG or DBG_OUT) else "Internal"
    dbg_d = nc.dram_tensor("dbg_d", [4, P, 18000], F32, kind=DK).ap() if DBG else None
    dbg2_d = nc.dram_tensor("dbg2_d", [4, P, NT], F32, kind=DK).ap() if DBG else None
    dbg3_d = nc.dram_tensor("dbg3_d", [P, NT], I32, kind=DK).ap() if DBG else None
    dbg4_d = nc.dram_tensor("dbg4_d", [2, P, 8 * SEQ], F32, kind=DK).ap() if DBG else None
    dbg5_d = nc.dram_tensor("dbg5_d", [2, P, 8 * SEQ], F32, kind=DK).ap() if DBG else None

    xnew_d = nc.dram_tensor("xnew_d", [TLOC, D], F32, kind=DK).ap()
    xs_d = nc.dram_tensor("xs_d", [TLOC, D], BF16, kind=DK).ap()
    sc_d = nc.dram_tensor("sc_d", [8 * CAP + P, D], BF16).ap()
    rsout_d = nc.dram_tensor("rsout_d", [CAP, D], BF16).ap()
    agin_d = nc.dram_tensor("agin_d", [CAP, D], BF16).ap()
    agout_d = nc.dram_tensor("agout_d", [8 * CAP, D], BF16).ap()
    if DBG or DBG_OUT:
        rs_dump = nc.dram_tensor("rs_dump", [CAP, D], BF16, kind="ExternalOutput").ap()
        ag_dump = nc.dram_tensor("ag_dump", [8 * CAP, D], BF16, kind="ExternalOutput").ap()
        agi_dump = nc.dram_tensor("agi_dump", [CAP, D], BF16, kind="ExternalOutput").ap()
    misc_d = nc.dram_tensor("misc_d", [20, D], F32).ap()
    ccin_d = nc.dram_tensor("ccin_d", [1, 64], F32).ap()
    ccout_d = nc.dram_tensor("ccout_d", [1, 64], F32).ap()

    def sb(name, shape, dt=BF16):
        return es.enter_context(nc.sbuf_tensor(name, shape, dt))

    zeros_t = sb("zeros_t", [P, 512])
    idn = sb("idn", [P, P])
    tri = sb("tri", [P, P])
    masks = sb("masks", [P, 4 * 512])
    iota21 = sb("iota21", [P, 21], F32)
    ebias = sb("ebias", [P, 16 * 42], F32)
    padm = sb("padm", [P, 1], F32)
    m0rep = sb("m0rep", [P, 1], F32)
    qs8 = sb("qs8", [P, 1], F32)
    ksr = sb("ksr", [P, 1], F32)
    gcol = sb("gcol", [P, 8], F32)
    wsw = sb("wsw", [P, 64])
    bswrow = sb("bswrow", [1, E], F32)
    bswar = sb("bswar", [P, E], F32)
    e07 = sb("e07", [P, E], F32)
    ones8 = sb("ones8", [P, E], F32)
    b1s = sb("b1s", [P, FF // P], F32)
    b2c = sb("b2c", [P, E], F32)
    onesc = sb("onesc", [1, P])
    onescf = sb("onescf", [1, P], F32)
    ones128 = sb("ones128", [P, 1])
    e_sel = sb("e_sel", [P, 2])
    f_sel = sb("f_sel", [2, P])
    sar = sb("sar", [P, BL * D])
    ca1r = sb("ca1r", [P, BL * D])
    sbr = sb("sbr", [P, BL * D])
    cb1r = sb("cb1r", [P, BL * D])
    gar = sb("gar", [P, BL * D])
    cT = sb("cT", [P, 8 * BL])
    m0sc = sb("m0sc", [1, 8], F32)
    xt0 = sb("xt0", [P, D], F32)
    xt1 = sb("xt1", [P, D], F32)
    d1t = sb("d1t", [P, D], F32)
    hmt = sb("hmt", [P, D])
    xnt = sb("xnt", [P, D])
    s1a = sb("s1a", [P, 1], F32)
    s1b = sb("s1b", [P, 1], F32)
    s1c = sb("s1c", [P, 1], F32)
    s1d = sb("s1d", [P, 1], F32)
    stats2 = sb("stats2", [P, 2 * NT], F32)
    xnT = sb("xnT", [P, 8 * SEQ])
    kh = sb("kh", [P, 8 * KC])
    vt = sb("vt", [P, NKCH * H * 65])
    qh = sb("qh", [P, 8 * 512])
    wqst = sb("wqst", [P, 2 * 1024])
    wkv_sb = sb("wkv_sb", [P, 8 * D])
    wo_sb = sb("wo_sb", [P, 8 * D])
    aoT = sb("aoT", [P, 8 * 512])
    sqt = sb("sqt", [P, 512])
    rn16 = sb("rn16", [2, 512])
    rnrep = sb("rnrep", [P, 512])
    nsq = sb("nsq", [P, 2])
    nrn = sb("nrn", [2, 2])
    nrnr = sb("nrnr", [P, 2])
    pt_sb = sb("pt_sb", [P, 2 * 512])
    routes_sb = sb("routes_sb", [P, NT], F32)
    oh_sb = sb("oh_sb", [P, NT * E])
    ohb_sb = sb("ohb_sb", [P, E])
    posl_sb = sb("posl_sb", [P, NT], F32)
    r8a = sb("r8a", [P, E], F32)
    r8b = sb("r8b", [P, E], F32)
    cnt8 = sb("cnt8", [1, E], F32)
    cnt64 = sb("cnt64", [1, 64], F32)
    cmask_sb = sb("cmask_sb", [1, 64])
    pmask_sb = sb("pmask_sb", [1, 64])
    base8 = sb("base8", [1, E], F32)
    basear = sb("basear", [P, E], F32)
    prefrow = sb("prefrow", [1, E], F32)
    keepf = sb("keepf", [P, 1], F32)
    slot_f = sb("slot_f", [P, 1], F32)
    slot_i = sb("slot_i", [P, NT], I32)
    gslot_i = sb("gslot_i", [P, NT], I32)
    t2t = sb("t2t", [P, 1], F32)
    t3t = sb("t3t", [P, 1], F32)
    dl1 = sb("dl1", [P, D])
    junk_t = dl1

    sprime = xt1[:, 0:512]
    wmod_sb = aoT
    scal1 = cnt64[0:1, 0:64]
    scal2 = ebias[0:1, 0:64]
    mev = sprime
    gbr = masks
    xsf_t = d1t
    xs_sb = hmt
    csil = hmt
    c_sb = d1t
    xsT = xnt
    gst = gar[:, 0:D]
    recrow = rn16
    rrS = rnrep
    def oTs(dcn):
        t = sar if dcn < 4 else ca1r
        return t[:, (dcn % 4) * 512:(dcn % 4 + 1) * 512]
    uT = xnT
    bufT = qh
    bufrow = aoT
    wmod_alias_check = True
    def w2slice(pch, dc):
        if pch < 17:
            return kh[:, pch * D + dc * P: pch * D + (dc + 1) * P]
        q = pch - 17
        return vt[:, q * D + dc * P: q * D + (dc + 1) * P]

    pb = [es.enter_context(nc.psum_tensor(f"pb{i}", [P, 512], F32)) for i in range(6)]
    pt6 = es.enter_context(nc.psum_tensor("pt6", [P, D], BF16))
    pt7 = es.enter_context(nc.psum_tensor("pt7", [P, D], BF16))

    for s in ["qS", "qA", "qG", "qW", "qI", "qD0", "qD1", "qD2", "qD3", "qD4", "qD5", "qD6", "qD7", "qD8", "qD9", "qDA", "qDB", "qDC", "qDD", "eV", "eP", "eS", "eG", "cc"]:
        kb.sem(s)

    kb.dma("sync", c_sb[0:2, :], c_in[:, :], "qS")
    kb.dma("sync", scal1[0:1, :], qs_in.rearrange("(a d) -> a d", a=1), "qS")
    kb.dma("sync", scal2[0:1, :], ks_in.rearrange("(a d) -> a d", a=1), "qS")
    kb.dma("sync", bswrow[0:1, :], bsw_in.rearrange("(a e) -> a e", a=1), "qS")
    kb.dma("sync", b1s[:, :], b1_in.rearrange("(c p) -> p c", p=P), "qS")
    kb.dma("sync", b2c[:, :], b2_in.rearrange("(c p) -> p c", p=P), "qS")
    kb.dma("gpsimd", cmask_sb[0:1, :], cmask_in[:, :], "qW")
    kb.dma("gpsimd", pmask_sb[0:1, :], pmask_in[:, :], "qW")
    for half in range(2):
        kb.dma("sync", qs8[half * 64:(half + 1) * 64, 0:1],
               qs_in.rearrange("(d a) -> d a", a=1), "qS")
        kb.dma("sync", ksr[half * 64:(half + 1) * 64, 0:1],
               ks_in.rearrange("(d a) -> d a", a=1), "qS")
    ld0 = kb.n("qS")
    kb.dma("gpsimd", wsw[:, :].rearrange("p (c e) -> p c e", c=8), wsw_in.rearrange("(c p) e -> p c e", p=P), "qW")
    kb.dma("gpsimd", gcol[:, :], gamma_in.rearrange("(c p) -> p c", p=P), "qG")
    gld0 = kb.n("qG")
    gld0w = kb.n("qW")

    def vmem(ap, v, sname=None):
        r = kb.op("vector", lambda e, a=ap, vv=v: e.memset(a, vv), "eV")
        kb.wait("vector", "eV", r)
        return r
    vmem(zeros_t[:, :], 0.0)
    vmem(idn[:, :], 0.0)
    vmem(tri[:, :], 0.0)
    vmem(masks[:, :], 0.0)
    vmem(onesc[:, :], 1.0)
    vmem(onescf[:, :], 1.0)
    vmem(ones128[:, :], 1.0)
    vmem(ones8[:, :], 1.0)
    vmem(padm[:, :], 0.0)
    vmem(prefrow[0:1, :], 0.0)
    for ee in range(8):
        vmem(e07[:, ee:ee + 1], float(ee))
    vmem(e_sel[:, :], 1.0)
    vmem(f_sel[0:2, :], 1.0)
    vm0 = kb.op("vector", lambda e: e.engine_nop(), "eV")
    if P0CUT <= 0:
        kb.wait("sync", "qS")
        kb.wait("sync", "qA")
        kb.wait("sync", "qG")
        kb.wait("sync", "qW")
        kb.wait("sync", "qI")
        kb.build()
        es.close()
        return nc

    kb.wait("gpsimd", "eV", vm0)
    kb.op("gpsimd", lambda e: e.affine_select(out=idn[:, :], in_=idn[:, :],
          compare_op=ALU.not_equal, fill=1.0, base=0, pattern=[[-1, P]], channel_multiplier=1), "eG")
    kb.op("gpsimd", lambda e: e.affine_select(out=tri[:, :], in_=tri[:, :],
          compare_op=ALU.is_ge, fill=1.0, base=0, pattern=[[-1, P]], channel_multiplier=1), "eG")
    for m in range(4):
        kb.op("gpsimd", lambda e, m=m: e.affine_select(
            out=masks[:, m * 512:(m + 1) * 512], in_=masks[:, m * 512:(m + 1) * 512],
            compare_op=ALU.is_ge, fill=NEG, base=-128 * m, pattern=[[1, 512]],
            channel_multiplier=-1), "eG")
    kb.op("gpsimd", lambda e: e.affine_select(out=padm[:, :], in_=padm[:, :],
          compare_op=ALU.is_ge, fill=NEG, base=-126, pattern=[[0, 1]], channel_multiplier=1), "eG")
    kb.op("gpsimd", lambda e: e.iota(iota21[:, :], pattern=[[-128, 21]], base=384,
          channel_multiplier=1, allow_small_or_imprecise_dtypes=True), "eG")
    kb.op("gpsimd", lambda e: e.affine_select(out=e_sel[:, :], in_=e_sel[:, :],
          compare_op=ALU.is_ge, fill=0.0, base=0, pattern=[[-64, 2]], channel_multiplier=1), "eG")
    kb.wait("gpsimd", "eG")
    kb.op("gpsimd", lambda e: e.affine_select(out=e_sel[:, :], in_=e_sel[:, :],
          compare_op=ALU.is_ge, fill=0.0, base=63, pattern=[[64, 2]], channel_multiplier=-1), "eG")
    kb.op("gpsimd", lambda e: e.affine_select(out=f_sel[0:2, :], in_=f_sel[0:2, :],
          compare_op=ALU.is_ge, fill=0.0, base=0, pattern=[[1, P]], channel_multiplier=-64), "eG")
    kb.wait("gpsimd", "eG")
    kb.op("gpsimd", lambda e: e.affine_select(out=f_sel[0:2, :], in_=f_sel[0:2, :],
          compare_op=ALU.is_ge, fill=0.0, base=63, pattern=[[-1, P]], channel_multiplier=64), "eG")
    g_pre = kb.n("eG")
    if P0CUT <= 1:
        kb.wait("sync", "qS")
        kb.wait("sync", "qA")
        kb.wait("sync", "qG")
        kb.wait("sync", "qW")
        kb.wait("sync", "qI")
        kb.build()
        es.close()
        return nc

    kb.wait("vector", "qS", ld0)
    kb.op_sync("vector", lambda e: e.reduce_max(m0sc[0:1, 0:1], scal1[0:1, :], axis=AX.X,
                                           apply_absolute_value=True))
    kb.op_sync("vector", lambda e: e.reduce_max(m0sc[0:1, 1:2], scal2[0:1, :], axis=AX.X,
                                           apply_absolute_value=True))
    kb.op("vector", lambda e: e.tensor_tensor(m0sc[0:1, 2:3], m0sc[0:1, 0:1], m0sc[0:1, 1:2],
                                              ALU.mult))
    m0v = kb.op("vector", lambda e: e.tensor_single_scalar(m0sc[0:1, 3:4], m0sc[0:1, 2:3],
                                                           8.0, ALU.mult), "eV")
    kb.wait("sync", "eV", m0v)
    kb.dma("sync", misc_d[12:13, 0:4], m0sc[0:1, 0:4], "qS")
    kb.wait("sync", "qS")
    kb.dma("sync", m0rep[:, :], misc_d[12:13, 3:4].partition_broadcast(P), "qS")
    m0ld = kb.n("qS")
    kb.wait("vector", "qS", m0ld)
    qs8v = kb.op("vector", lambda e: e.tensor_single_scalar(qs8[:, :], qs8[:, :], 8.0, ALU.mult),
                 "eV")

    kb.wait("vector", "eG", g_pre)
    for h in range(H):
        base = h * 42
        ctr = slopes[h] * (qtw_of(h) / 2.0)
        kb.op("vector", lambda e, h=h, base=base: e.tensor_single_scalar(
            ebias[:, base:base + 21], iota21[:, :], slopes[h], ALU.mult))
        kb.op("vector", lambda e, base=base, ctr=ctr: e.tensor_single_scalar(
            ebias[:, base:base + 21], ebias[:, base:base + 21], -ctr, ALU.add))
        kb.op("vector", lambda e, base=base: e.tensor_scalar(
            ebias[:, base:base + 21], ebias[:, base:base + 21], m0rep[:, 0:1], None, ALU.subtract))
        kb.op("vector", lambda e, base=base: e.tensor_scalar(
            ebias[:, base + 21:base + 42], ebias[:, base:base + 21], padm[:, 0:1], None, ALU.add))
    b1v = kb.op("vector", lambda e: e.tensor_single_scalar(b1s[:, :], b1s[:, :], GELU_S, ALU.mult),
                "eV")
    ebias_v = b1v
    if P0CUT <= 2:
        kb.wait("sync", "qS")
        kb.wait("sync", "qA")
        kb.wait("sync", "qG")
        kb.wait("sync", "qW")
        kb.wait("sync", "qI")
        kb.build()
        es.close()
        return nc

    kb.wait("scalar", "qS", ld0)
    sil_v = kb.op("scalar", lambda e: e.activation(out=csil[0:2, :], in_=c_sb[0:2, :], func=AF.Silu),
                  "eS")
    kb.wait("tensor", "eS", sil_v)
    kb.wait("tensor", "eG", g_pre)
    for ch in range(8):
        kb.op("tensor", lambda e, ch=ch: e.transpose(
            pt6[:, ch * 2:(ch + 1) * 2], csil[0:2, ch * P:(ch + 1) * P], idn[0:2, 0:2]),
            "eP" if ch == 7 else None)
    tr_c = kb.n("eP")
    kb.wait("vector", "eP", tr_c)
    cT_v = kb.op("vector", lambda e: e.tensor_copy(cT[:, :], pt6[:, 0:16]), "eV")
    nsl_done = [None] * 12
    mm_prev = [None]
    for ns in range(12):
        if ns >= 1:
            kb.wait("gpsimd", "eP", mm_prev[0])
        for dchunk in range(8):
            kb.dma("gpsimd", wmod_sb[:, dchunk * 512:(dchunk + 1) * 512],
                   wmod_in[dchunk * P:(dchunk + 1) * P, ns * 512:(ns + 1) * 512], "qW")
        ldv = kb.n("qW")
        kb.wait("tensor", "qW", ldv)
        if ns == 0:
            kb.wait("tensor", "eV", cT_v)
        if ns >= 1:
            kb.wait("tensor", "eV", nsl_done[ns - 1])
        for dchunk in range(8):
            kb.op("tensor", lambda e, ch=dchunk: e.matmul(
                pb[4][0:2, :], cT[:, ch * 2:(ch + 1) * 2],
                wmod_sb[:, ch * 512:(ch + 1) * 512],
                start=(ch == 0), stop=(ch == 7)), "eP" if ch == 7 else None)
        mm_prev[0] = kb.n("eP")
        mmv = kb.n("eP")
        kb.wait("vector", "eP", mmv)
        nsl_done[ns] = kb.op("vector", lambda e: e.tensor_copy(mev[0:2, :], pb[4][0:2, :]), "eV")
        kb.wait("sync", "eV", nsl_done[ns])
        kb.dma("sync", misc_d[2 * (ns // 2):2 * (ns // 2) + 2,
                              (ns % 2) * 512:(ns % 2 + 1) * 512], mev[0:2, :], "qS")
        kb.wait("vector", "qS")
    kb.wait("sync", "qS")
    kb.wait("gpsimd", "qS", kb.n("qS"))
    for b in range(BL):
        kb.dma("gpsimd", sar[:, b * D:(b + 1) * D],
               misc_d[0 + b:1 + b, :].partition_broadcast(P), "qW")
        kb.dma("gpsimd", ca1r[:, b * D:(b + 1) * D],
               misc_d[2 + b:3 + b, :].partition_broadcast(P), "qW")
        kb.dma("gpsimd", gar[:, b * D:(b + 1) * D],
               misc_d[4 + b:5 + b, :].partition_broadcast(P), "qW")
        kb.dma("gpsimd", sbr[:, b * D:(b + 1) * D],
               misc_d[6 + b:7 + b, :].partition_broadcast(P), "qW")
        kb.dma("gpsimd", cb1r[:, b * D:(b + 1) * D],
               misc_d[8 + b:9 + b, :].partition_broadcast(P), "qW")
    mods_ld_S = kb.n("qS")
    mods_ld_G = kb.n("qW")
    kb.wait("vector", "qW", mods_ld_G)
    kb.wait("vector", "qS", mods_ld_S)
    kb.op("vector", lambda e: e.tensor_single_scalar(ca1r[:, :], ca1r[:, :], 1.0, ALU.add))
    kb.op("vector", lambda e: e.tensor_single_scalar(cb1r[:, :], cb1r[:, :], 1.0, ALU.add))
    def g_transform(gt):
        ga_ = d1t[:, 0:512]
        gb_ = d1t[:, 512:1024]
        last = None
        for half in range(BL * D // 512):
            sl = (half * 512, (half + 1) * 512)
            kb.op("vector", lambda e, gt=gt, sl=sl: e.tensor_copy(ga_, gt[:, sl[0]:sl[1]]))
            kb.op("vector", lambda e: e.tensor_single_scalar(gb_, ga_, 0.0, ALU.is_ge))
            kb.op("vector", lambda e: e.tensor_scalar(gb_, gb_, 2.0, -1.0, ALU.mult, ALU.add))
            kb.op("vector", lambda e: e.tensor_tensor(ga_, ga_, gb_, ALU.mult))
            kb.op("vector", lambda e: e.tensor_single_scalar(ga_, ga_, 0.25, ALU.max))
            last = kb.op("vector", lambda e, gt=gt, sl=sl: e.tensor_tensor(
                gt[:, sl[0]:sl[1]], ga_, gb_, ALU.mult), "eV")
        return last
    modtr_v = g_transform(gar)
    if P0CUT <= 3:
        kb.wait("sync", "qS")
        kb.wait("sync", "qA")
        kb.wait("sync", "qG")
        kb.wait("sync", "qW")
        kb.wait("sync", "qI")
        kb.build()
        es.close()
        return nc

    kb.wait("vector", "qS", ld0)
    bsw_rdy = kb.op("vector", lambda e: e.engine_nop(), "eV")
    kb.wait("tensor", "eV", bsw_rdy)
    bswv = kb.op("tensor", lambda e: e.matmul(pb[4][:, 0:E], onescf[0:1, :], bswrow[0:1, :],
                                              start=True, stop=True), "eP")
    kb.wait("vector", "eP", bswv)
    bswcv = kb.op("vector", lambda e: e.tensor_copy(bswar[:, :], pb[4][:, 0:E]), "eV")

    if P0CUT <= 4:
        kb.wait("sync", "qS")
        kb.wait("sync", "qA")
        kb.wait("sync", "qG")
        kb.wait("sync", "qW")
        kb.wait("sync", "qI")
        kb.build()
        es.close()
        return nc
    kb.wait("scalar", "eV", vm0)
    for r in range(0, 8 * CAP + P, P):
        kb.dma("scalar", sc_d[r:r + P, 0:512], zeros_t[:, :], "qA")
        kb.dma("scalar", sc_d[r:r + P, 512:D], zeros_t[:, :], "qA")
    sc_zeroed = kb.n("qA")

    if P0CUT <= 5:
        kb.wait("sync", "qS")
        kb.wait("sync", "qA")
        kb.wait("sync", "qG")
        kb.wait("sync", "qW")
        kb.wait("sync", "qI")
        kb.build()
        es.close()
        return nc
    for ch in range(8):
        kb.dma("gpsimd", wo_sb[:, ch * D:(ch + 1) * D], wout_in[ch * P:(ch + 1) * P, :], "qW")
    w_ld = kb.n("qW")
    wqv = wq_in.rearrange("(dc p) (kc c) -> kc p dc c", p=P, c=P)

    if PHASES < 1:
        kb.wait("sync", "qS")
        kb.wait("sync", "qA")
        kb.wait("sync", "qG")
        kb.wait("sync", "qW")
        kb.wait("sync", "qI")
        kb.build()
        es.close()
        return nc
    def ln_half(xsrc, dst_bf, mulrep, addrep, want_stats):
        kb.op_sync("vector", lambda e: e.reduce_sum(s1a[:, :], xsrc[:, :], axis=AX.X))
        kb.op("vector", lambda e: e.tensor_single_scalar(s1a[:, :], s1a[:, :], 1.0 / D, ALU.mult))
        dv = kb.op("vector", lambda e: e.tensor_scalar(d1t[:, :], xsrc[:, :], s1a[:, 0:1],
                                                       None, ALU.subtract), "eV")
        kb.wait("scalar", "eV", dv)
        sv = kb.op("scalar", lambda e: e.activation(out=junk_t[:, :], in_=d1t[:, :],
                                                    func=AF.Square, accum_out=s1b[:, :]), "eS")
        kb.wait("vector", "eS", sv)
        kb.op("vector", lambda e: e.tensor_scalar(s1b[:, :], s1b[:, :], 1.0 / D, 1e-5,
                                                  ALU.mult, ALU.add))
        rv = kb.op_sync("vector", lambda e: e.reciprocal(s1c[:, :], s1b[:, :]), "eV")
        kb.wait("scalar", "eV", rv)
        rsv = kb.op("scalar", lambda e: e.activation(out=s1d[:, :], in_=s1c[:, :], func=AF.Sqrt),
                    "eS")
        kb.wait("vector", "eS", rsv)
        return dv

    seq_end_v = None
    for s in range(2):
        b = s
        kb.wait("vector", "eP", kb.n("eP"))
        ln_done = None
        xt_free = [None, None]
        for tl in range(NTS):
            tile = s * NTS + tl
            xt = xt0 if tl % 2 == 0 else xt1
            if xt_free[tl % 2] is not None:
                kb.wait("sync", "eV", xt_free[tl % 2])
            kb.dma("sync", xt[:, :], x_in[b, tl * P:(tl + 1) * P, :], "qS")
            xldv = kb.n("qS")
            kb.wait("vector", "qS", xldv)
            if tl == 0:
                kb.wait("vector", "qW", mods_ld_G)
                kb.wait("vector", "qS", mods_ld_S)
            xt_free[tl % 2] = ln_half(xt, None, None, None, False)
            kb.op("vector", lambda e: e.tensor_scalar(hmt[:, :], d1t[:, :], s1d[:, 0:1],
                                                      None, ALU.mult))
            kb.op("vector", lambda e, b=b: e.tensor_tensor(hmt[:, :], hmt[:, :],
                                                           ca1r[:, b * D:(b + 1) * D], ALU.mult))
            kb.op("vector", lambda e, b=b: e.tensor_tensor(hmt[:, :], hmt[:, :],
                                                           sar[:, b * D:(b + 1) * D], ALU.add))
            ln_half(hmt, None, None, None, False)
            xnv = kb.op("vector", lambda e: e.tensor_scalar(xnt[:, :], d1t[:, :], s1d[:, 0:1],
                                                            None, ALU.mult), "eV")
            kb.wait("tensor", "eV", xnv)
            if tl == 0 and seq_end_v is not None:
                kb.wait("tensor", "eV", seq_end_v)
            for ch in range(8):
                kb.op("tensor", lambda e, ch=ch: e.transpose(
                    pt6[:, ch * P:(ch + 1) * P], xnt[:, ch * P:(ch + 1) * P], idn[:, :]),
                    "eP" if ch == 7 else None)
            tv = kb.n("eP")
            kb.wait("vector", "eP", tv)
            ln_done = kb.op("vector", lambda e, tl=tl: e.tensor_copy(
                xnT[:, :].rearrange("p (c q) -> p c q", c=8)[:, :, tl * P:(tl + 1) * P],
                pt6[:, :].rearrange("p (c q) -> p c q", c=8)), "eV")
        xn_ready = ln_done

        kb.wait("vector", "eV", xn_ready)
        r = kb.op("vector", lambda e: e.memset(kh[:, :], 0.0), "eV")
        kb.wait("vector", "eV", r)
        r = kb.op("vector", lambda e: e.memset(vt[:, :], 0.0), "eV")
        kb.wait("vector", "eV", r)
        for t17 in range(NKCH):
            r = kb.op("vector", lambda e, t=t17: e.memset(
                vt[:, :].rearrange("p (t h c) -> p t h c", t=NKCH, h=H)[:, t, :, 64:65], 1.0),
                "eV")
        kb.wait("vector", "eV", r)
        kv_init = kb.op("vector", lambda e: e.engine_nop(), "eV")
        kb.wait("gpsimd", "eV", kv_init)
        for kc in range(8):
            for hl in range(2):
                h = 2 * kc + hl
                kb.dma("gpsimd", kh[hl * 64:(hl + 1) * 64, kc * KC + 126:kc * KC + 128],
                       nullkv_in.rearrange("h (i k) d -> h i k d", k=2)[h, :, 0, :]
                       .rearrange("i d -> d i"), "qW")
        for h in range(H):
            kb.dma("gpsimd", vt[126:128, h * 65:h * 65 + 64],
                   nullkv_in.rearrange("h (i k) d -> h i k d", k=2)[h, :, 1, :], "qW")
        null_ld = kb.n("qW")

        chain_prev = [None]
        def norms_chain(bank, cols, kc, dst, off, scale_col, mm_done):
            if chain_prev[0] is not None:
                kb.wait("scalar", "eV", chain_prev[0])
            kb.wait("scalar", "eP", mm_done)
            sv = kb.op("scalar", lambda e, s=bank, c=cols: e.activation(
                out=sqt[:, 0:c], in_=s[:, 0:c], func=AF.Square), "eS")
            kb.wait("tensor", "eS", sv)
            n2v = kb.op("tensor", lambda e, c=cols: e.matmul(
                pb[3][0:2, 0:c], e_sel[:, 0:2], sqt[:, 0:c],
                start=True, stop=True), "eP")
            kb.wait("vector", "eP", n2v)
            mv = kb.op("vector", lambda e, c=cols: e.tensor_single_scalar(
                rn16[0:2, 0:c], pb[3][0:2, 0:c], 1e-24, ALU.max), "eV")
            kb.wait("scalar", "eV", mv)
            qv = kb.op("scalar", lambda e, c=cols: e.activation(
                out=rn16[0:2, 0:c], in_=rn16[0:2, 0:c], func=AF.Sqrt), "eS")
            kb.wait("vector", "eS", qv)
            rv = kb.op_sync("vector", lambda e, c=cols: e.reciprocal(rn16[0:2, 0:c], rn16[0:2, 0:c]),
                       "eV")
            kb.wait("tensor", "eV", rv)
            fv = kb.op("tensor", lambda e, c=cols: e.matmul(
                pb[3][:, 0:c], f_sel[0:2, 0:P], rn16[0:2, 0:c],
                start=True, stop=True), "eP")
            kb.wait("vector", "eP", fv)
            kb.op("vector", lambda e, c=cols: e.tensor_copy(rnrep[:, 0:c], pb[3][:, 0:c]))
            kb.op("vector", lambda e, s=bank, c=cols, sc=scale_col: e.tensor_scalar(
                sqt[:, 0:c], s[:, 0:c], sc, None, ALU.mult))
            done = kb.op("vector", lambda e, c=cols, d=dst, o=off: e.tensor_tensor(
                d[:, o:o + c], sqt[:, 0:c], rnrep[:, 0:c], ALU.mult), "eV")
            chain_prev[0] = done
            return done

        if s > 0:
            kb.wait("gpsimd", "eP", kb.n("eP"))
        for ch in range(8):
            kb.dma("gpsimd", wkv_sb[:, ch * D:(ch + 1) * D],
                   wkv_in[ch * P:(ch + 1) * P, 0:D], "qW")
        wk_ldv = kb.n("qW")
        kb.wait("vector", "qW", wk_ldv)
        kb.wait("vector", "qG", gld0)
        for ch in range(8):
            kb.op("vector", lambda e, ch=ch: e.tensor_scalar(
                wkv_sb[:, ch * D:(ch + 1) * D], wkv_sb[:, ch * D:(ch + 1) * D],
                gcol[:, ch:ch + 1], None, ALU.mult), "eV" if ch == 7 else None)
        wk_scv = kb.n("eV")
        kb.wait("tensor", "qW", w_ld)
        kb.wait("tensor", "eV", wk_scv)
        kb.wait("tensor", "eV", kv_init)
        bank_free = [None, None]
        for kc in range(8):
            for ts in range(4):
                bi = (kc * 4 + ts) % 2
                bank = pb[bi]
                if bank_free[bi] is not None:
                    kb.wait("tensor", "eV", bank_free[bi])
                for dc in range(8):
                    kb.op("tensor", lambda e, dc=dc, kc=kc, ts=ts, bank=bank: e.matmul(
                        bank[:, :], wkv_sb[:, dc * D + kc * P: dc * D + (kc + 1) * P],
                        xnT[:, :].rearrange("p (c q) -> p c q", c=8)[:, dc,
                                                                    ts * 512:(ts + 1) * 512],
                        start=(dc == 0), stop=(dc == 7)), "eP" if dc == 7 else None)
                mmv = kb.n("eP")
                bank_free[bi] = norms_chain(bank, 512, kc, kh, kc * KC + KOFF + ts * 512,
                                            ksr[:, 0:1], mmv)
        for kc in range(8):
            if chain_prev[0] is not None:
                kb.wait("vector", "eV", chain_prev[0])
            kb.wait("vector", "qW", null_ld)
            nv = kb.op("vector", lambda e, kc=kc: e.tensor_tensor(
                nsq[:, :], kh[:, kc * KC + 126:kc * KC + 128],
                kh[:, kc * KC + 126:kc * KC + 128], ALU.mult), "eV")
            kb.wait("tensor", "eV", nv)
            nv2 = kb.op("tensor", lambda e: e.matmul(
                pb[3][0:2, 0:2], e_sel[:, 0:2], nsq[:, :],
                start=True, stop=True), "eP")
            kb.wait("vector", "eP", nv2)
            nv3 = kb.op("vector", lambda e: e.tensor_single_scalar(
                nrn[0:2, :], pb[3][0:2, 0:2], 1e-24, ALU.max), "eV")
            kb.wait("scalar", "eV", nv3)
            nv4 = kb.op("scalar", lambda e: e.activation(out=nrn[0:2, :], in_=nrn[0:2, :],
                                                         func=AF.Sqrt), "eS")
            kb.wait("vector", "eS", nv4)
            nv5 = kb.op_sync("vector", lambda e: e.reciprocal(nrn[0:2, :], nrn[0:2, :]), "eV")
            kb.wait("tensor", "eV", nv5)
            nv6 = kb.op("tensor", lambda e: e.matmul(
                pb[3][:, 0:2], f_sel[0:2, 0:P], nrn[0:2, :],
                start=True, stop=True), "eP")
            kb.wait("vector", "eP", nv6)
            kb.op("vector", lambda e: e.tensor_copy(nrnr[:, :], pb[3][:, 0:2]))
            kb.op("vector", lambda e, kc=kc: e.tensor_scalar(
                nsq[:, :], kh[:, kc * KC + 126:kc * KC + 128], ksr[:, 0:1], None, ALU.mult))
            chain_prev[0] = kb.op("vector", lambda e, kc=kc: e.tensor_tensor(
                kh[:, kc * KC + 126:kc * KC + 128], nsq[:, :], nrnr[:, :], ALU.mult), "eV")

        kb.wait("gpsimd", "eP", kb.n("eP"))
        for ch in range(8):
            kb.dma("gpsimd", wkv_sb[:, ch * D:(ch + 1) * D],
                   wkv_in[ch * P:(ch + 1) * P, D:2 * D], "qW")
        wv_ldv = kb.n("qW")
        kb.wait("vector", "qW", wv_ldv)
        for ch in range(8):
            kb.op("vector", lambda e, ch=ch: e.tensor_scalar(
                wkv_sb[:, ch * D:(ch + 1) * D], wkv_sb[:, ch * D:(ch + 1) * D],
                gcol[:, ch:ch + 1], None, ALU.mult), "eV" if ch == 7 else None)
        wv_scv = kb.n("eV")
        kb.wait("tensor", "eV", wv_scv)
        vbank_free = [None, None]
        for tl in range(NTS):
            bi = tl % 2
            bank = pb[bi]
            if vbank_free[bi] is not None:
                kb.wait("tensor", "eV", vbank_free[bi])
            if tl == 0:
                kb.wait("tensor", "eV", chain_prev[0])
            bank2 = pb[4] if bi == 0 else pb[3]
            for dc in range(8):
                for half in range(2):
                    tgt = bank if half == 0 else bank2
                    kb.op("tensor", lambda e, dc=dc, tl=tl, half=half, tgt=tgt: e.matmul(
                        tgt[:, :],
                        xnT[:, :].rearrange("p (c q) -> p c q", c=8)[:, dc, tl * P:(tl + 1) * P],
                        wkv_sb[:, dc * D + half * 512:dc * D + (half + 1) * 512],
                        start=(dc == 0), stop=(dc == 7)),
                        "eP" if (dc == 7 and half == 1) else None)
            vv = kb.n("eP")
            kb.wait("vector", "eP", vv)
            kb.op("vector", lambda e, tl=tl, bank=bank: e.tensor_copy(
                vt[:, :].rearrange("p (t h c) -> p t h c", t=NKCH, h=H)[:, tl + 1, 0:8, 0:64],
                bank[:, :].rearrange("p (h c) -> p h c", h=8)))
            vbank_free[bi] = kb.op("vector", lambda e, tl=tl, bank2=bank2: e.tensor_copy(
                vt[:, :].rearrange("p (t h c) -> p t h c", t=NKCH, h=H)[:, tl + 1, 8:16, 0:64],
                bank2[:, :].rearrange("p (h c) -> p h c", h=8)), "eV")
        kv_ready = vbank_free[(NTS - 1) % 2]

        if DBG and s == 0:
            kb.wait("gpsimd", "eV", kv_ready)
            kb.dma("gpsimd", dbg_d[0, :, 0:8 * KC], kh[:, :], "qD0")
            kb.dma("gpsimd", dbg_d[3, :, 0:NKCH * H * 65], vt[:, :], "qD1")
            kb.dma("gpsimd", dbg4_d[0], xnT[:, :], "qD8")
            kb.dma("gpsimd", dbg4_d[1, :, 0:8 * D], wkv_sb[:, :], "qD9")
            kb.wait("vector", "qD0", 16)
            kb.wait("vector", "qD1", 16)
            kb.wait("vector", "qD8", 16)
            kb.wait("vector", "qD9", 16)
            kb.wait("tensor", "qD1", 16)

        sup_gate = kv_ready
        for qs_i in range(4):
            kb.wait("tensor", "eV", sup_gate)
            kb.wait("gpsimd", "eV", sup_gate)
            kb.wait("vector", "qS", kb.n("qS"))
            chain_prev[0] = None
            qbank_free = [None, None]
            wq_ld = [None] * 8
            wq_used = [None, None]
            wq_sc = [None] * 8
            def load_wq(kc):
                if wq_used[kc % 2] is not None:
                    kb.wait("gpsimd", "eP", wq_used[kc % 2])
                kb.dma("gpsimd", wqst[:, (kc % 2) * 1024:(kc % 2 + 1) * 1024], wqv[kc], "qW")
                wq_ld[kc] = kb.n("qW")
                kb.wait("vector", "qW", wq_ld[kc])
                for dc in range(8):
                    kb.op("vector", lambda e, kc=kc, dc=dc: e.tensor_scalar(
                        wqst[:, (kc % 2) * 1024 + dc * P:(kc % 2) * 1024 + (dc + 1) * P],
                        wqst[:, (kc % 2) * 1024 + dc * P:(kc % 2) * 1024 + (dc + 1) * P],
                        gcol[:, dc:dc + 1], None, ALU.mult), "eV" if dc == 7 else None)
                wq_sc[kc] = kb.n("eV")
            load_wq(0)
            load_wq(1)
            for kc in range(8):
                bi = kc % 2
                bank = pb[bi]
                if qbank_free[bi] is not None:
                    kb.wait("tensor", "eV", qbank_free[bi])
                kb.wait("tensor", "eV", wq_sc[kc])
                for dc in range(8):
                    kb.op("tensor", lambda e, dc=dc, kc=kc, qs_i=qs_i, bank=bank: e.matmul(
                        bank[:, :],
                        wqst[:, (kc % 2) * 1024 + dc * P:(kc % 2) * 1024 + (dc + 1) * P],
                        xnT[:, :].rearrange("p (c q) -> p c q", c=8)[:, dc,
                                                                    qs_i * 512:(qs_i + 1) * 512],
                        start=(dc == 0), stop=(dc == 7)), "eP" if dc == 7 else None)
                mmv = kb.n("eP")
                wq_used[bi] = mmv
                if kc + 2 < 8:
                    load_wq(kc + 2)
                qbank_free[bi] = norms_chain(bank, 512, kc, qh, kc * 512, qs8[:, 0:1], mmv)
            kb.wait("tensor", "eV", qbank_free[0])
            kb.wait("tensor", "eV", qbank_free[1])
            if DBG and s == 0 and qs_i == 0:
                kb.wait("gpsimd", "eV", qbank_free[0])
                kb.wait("gpsimd", "eV", qbank_free[1])
                kb.dma("gpsimd", dbg_d[1, :, 0:8 * 512], qh[:, :], "qD2")
                kb.wait("vector", "qD2", 16)
                kb.wait("tensor", "qD2", 16)
            kb.wait("scalar", "qS", m0ld)
            kb.wait("scalar", "eV", ebias_v)

            kb.wait("vector", "qS")
            head_evac = None
            evac_hist = [None, None]
            for h in range(H):
                nbank = pb[2] if h % 2 == 0 else pb[4]
                if evac_hist[h % 2] is not None:
                    kb.wait("tensor", "eV", evac_hist[h % 2])
                qtw = qtw_of(h)
                kc_h = h // 2
                hh = h % 2
                ebase = h * 42
                nqt = 512 // qtw
                sbank_busy = [None, None]
                pt_busy = [None, None]
                spr_busy = [None, None]
                av_last = None
                all_chunks = []
                for qt in range(nqt):
                    i0 = qs_i * 512 + qt * qtw
                    all_chunks.append((qt, chunk_list(h, i0, qtw)))
                ci_g = 0
                for qt, c0s in all_chunks:
                    qoff = qt * qtw
                    i0 = qs_i * 512 + qt * qtw
                    for ci_l, c0 in enumerate(c0s):
                        bi = ci_g % 2
                        pt = ci_g % 2
                        sbank = pb[bi]
                        dc_units = (i0 + P - c0) // P
                        bcol = ebase + (21 if c0 == 0 else 0) + (dc_units + 3)
                        diag = c0 >= i0 + P
                        mvar = (c0 - i0 - P) // P if diag else 0
                        if sbank_busy[bi] is not None:
                            kb.wait("tensor", "eS", sbank_busy[bi][0])
                            if sbank_busy[bi][1] is not None:
                                kb.wait("tensor", "eV", sbank_busy[bi][1])
                        smv = kb.op("tensor", lambda e, kc=kc_h, hh=hh, c0=c0, qoff=qoff,
                                    qtw=qtw, sbank=sbank: e.matmul(
                                        sbank[:, 0:qtw],
                                        kh[hh * 64:(hh + 1) * 64, kc * KC + c0:kc * KC + c0 + P],
                                        qh[hh * 64:(hh + 1) * 64,
                                           kc * 512 + qoff:kc * 512 + qoff + qtw],
                                        start=True, stop=True), "eP")
                        if pt_busy[pt] is not None:
                            kb.wait("scalar", "eP", pt_busy[pt])
                        mv = None
                        if diag:
                            kb.wait("vector", "eP", smv)
                            if spr_busy[bi] is not None:
                                kb.wait("vector", "eS", spr_busy[bi])
                            sprh = xt1[:, bi * 512:bi * 512 + qtw]
                            mv = kb.op("vector", lambda e, sbank=sbank, qtw=qtw, m=mvar,
                                       sprh=sprh: e.tensor_tensor(
                                           sprh, sbank[:, 0:qtw],
                                           masks[:, m * 512:m * 512 + qtw], ALU.add),
                                       "eV")
                            kb.wait("scalar", "eV", mv)
                            ev = kb.op("scalar", lambda e, qtw=qtw, bcol=bcol, pt=pt, sprh=sprh:
                                       e.activation(out=pt_sb[:, pt * 512:pt * 512 + qtw],
                                                    in_=sprh, func=AF.Exp,
                                                    bias=ebias[:, bcol:bcol + 1]), "eS")
                            spr_busy[bi] = ev
                        else:
                            kb.wait("scalar", "eP", smv)
                            ev = kb.op("scalar", lambda e, sbank=sbank, qtw=qtw, bcol=bcol,
                                       pt=pt: e.activation(
                                           out=pt_sb[:, pt * 512:pt * 512 + qtw],
                                           in_=sbank[:, 0:qtw], func=AF.Exp,
                                           bias=ebias[:, bcol:bcol + 1]), "eS")
                        sbank_busy[bi] = (ev, mv)
                        kb.wait("tensor", "eS", ev)
                        av_last = kb.op("tensor", lambda e, c0=c0, h=h, qoff=qoff, qtw=qtw,
                                        st=(ci_l == 0), sp=(ci_l == len(c0s) - 1), pt=pt,
                                        nbank=nbank:
                                        e.matmul(nbank[0:65, qoff:qoff + qtw],
                                                 vt[:, :].rearrange("p (t z) -> p t z",
                                                                    t=NKCH)[:, c0 // P,
                                                                            h * 65:(h + 1) * 65],
                                                 pt_sb[:, pt * 512:pt * 512 + qtw],
                                                 start=st, stop=sp, skip_group_check=True),
                                        "eP")
                        pt_busy[pt] = av_last
                        ci_g += 1
                kb.wait("vector", "eP", av_last)
                rcv = kb.op_sync("vector", lambda e, nbank=nbank: e.reciprocal(
                    recrow[0:1, 0:512], nbank[64:65, :]), "eV")
                kb.wait("tensor", "eV", rcv)
                bcv = kb.op("tensor", lambda e: e.matmul(pb[3][0:64, :], onesc[0:1, 0:64],
                                                         recrow[0:1, 0:512], start=True, stop=True),
                            "eP")
                kb.wait("vector", "eP", bcv)
                kb.op("vector", lambda e: e.tensor_copy(rrS[0:64, 0:512], pb[3][0:64, :]))
                head_evac = kb.op("vector", lambda e, kc=kc_h, hh=hh, nbank=nbank: e.tensor_tensor(
                    aoT[hh * 64:(hh + 1) * 64, kc * 512:(kc + 1) * 512],
                    nbank[0:64, :], rrS[0:64, 0:512], ALU.mult), "eV")
                evac_hist[h % 2] = head_evac

            if DBG and s == 0 and qs_i == 0:
                kb.wait("gpsimd", "eV", head_evac)
                kb.dma("gpsimd", dbg_d[2, :, 0:8 * 512], aoT[:, :], "qD3")
                kb.wait("vector", "qD3", 16)
                kb.wait("tensor", "qD3", 16)

            kb.wait("tensor", "eV", head_evac)
            epi_v = None
            ep_x2v_prev = xt_free[0] if qs_i == 0 else None
            ep_store_prev = None
            for sub in range(4):
                tl = qs_i * 4 + sub
                tile = s * NTS + tl
                for dc in range(8):
                    for half in range(2):
                        kb.op("tensor", lambda e, dc=dc, sub=sub, half=half: e.matmul(
                            pb[half][:, :],
                            aoT[:, dc * 512 + sub * P:dc * 512 + (sub + 1) * P],
                            wo_sb[:, dc * D + half * 512:dc * D + (half + 1) * 512],
                            start=(dc == 0), stop=(dc == 7)),
                            "eP" if (dc == 7 and half == 1) else None)
                opv = kb.n("eP")
                if ep_x2v_prev is not None:
                    kb.wait("sync", "eV", ep_x2v_prev)
                kb.dma("sync", xt0[:, :], x_in[b, tl * P:(tl + 1) * P, :], "qS")
                xldv = kb.n("qS")
                kb.wait("vector", "eP", opv)
                kb.wait("vector", "qS", xldv)
                if ep_store_prev is not None:
                    kb.wait("vector", "qS", ep_store_prev)
                for half in range(2):
                    kb.op("vector", lambda e, half=half, b=b: e.tensor_tensor(
                        d1t[:, half * 512:(half + 1) * 512], pb[half][:, :],
                        gar[:, b * D + half * 512:b * D + (half + 1) * 512], ALU.mult))
                x2v = kb.op("vector", lambda e: e.tensor_tensor(xt1[:, :], d1t[:, :], xt0[:, :],
                                                                ALU.add), "eV")
                ep_x2v_prev = x2v
                kb.wait("sync", "eV", x2v)
                kb.dma("sync", xnew_d[tile * P:(tile + 1) * P, :], xt1[:, :], "qS")
                ln_half(xt1, None, None, None, True)
                kb.op("vector", lambda e, tile=tile: e.tensor_copy(stats2[:, tile:tile + 1],
                                                                   s1a[:, :]))
                kb.op("vector", lambda e, tile=tile: e.tensor_copy(
                    stats2[:, NT + tile:NT + tile + 1], s1d[:, :]))
                kb.op("vector", lambda e: e.tensor_scalar(xsf_t[:, :], d1t[:, :], s1d[:, 0:1],
                                                          None, ALU.mult))
                kb.op("vector", lambda e, b=b: e.tensor_tensor(
                    xsf_t[:, :], xsf_t[:, :], cb1r[:, b * D:(b + 1) * D], ALU.mult))
                xsv = kb.op("vector", lambda e, b=b: e.tensor_tensor(
                    xs_sb[:, :], xsf_t[:, :], sbr[:, b * D:(b + 1) * D], ALU.add), "eV")
                kb.wait("sync", "eV", xsv)
                kb.dma("sync", xs_d[tile * P:(tile + 1) * P, :], xs_sb[:, :], "qS")
                ep_store_prev = kb.n("qS")
                kb.wait("tensor", "eV", xsv)
                for ch in range(8):
                    kb.op("tensor", lambda e, ch=ch: e.transpose(
                        pt7[:, ch * P:(ch + 1) * P], xs_sb[:, ch * P:(ch + 1) * P], idn[:, :]),
                        "eP" if ch == 7 else None)
                txv = kb.n("eP")
                kb.wait("vector", "eP", txv)
                xstv = kb.op("vector", lambda e: e.tensor_copy(xsT[:, :], pt7[:, :]), "eV")
                kb.wait("tensor", "eV", xstv)
                if tile == 0:
                    kb.wait("tensor", "qW", gld0w)
                for ch in range(8):
                    kb.op("tensor", lambda e, ch=ch: e.matmul(
                        pb[4][:, 0:E], xsT[:, ch * P:(ch + 1) * P], wsw[:, ch * 8:(ch + 1) * 8],
                        start=(ch == 0), stop=(ch == 7)), "eP" if ch == 7 else None)
                lgv = kb.n("eP")
                kb.wait("vector", "eP", lgv)
                if tile == 0:
                    kb.wait("vector", "eV", bswcv)
                kb.op("vector", lambda e: e.tensor_tensor(r8a[:, :], pb[4][:, 0:E], bswar[:, :],
                                                          ALU.add))
                kb.op_sync("vector", lambda e: e.reduce_max(s1a[:, :], r8a[:, :], axis=AX.X))
                smv = kb.op("vector", lambda e: e.tensor_scalar(r8a[:, :], r8a[:, :], s1a[:, 0:1],
                                                                None, ALU.subtract), "eV")
                kb.wait("scalar", "eV", smv)
                sev = kb.op("scalar", lambda e: e.activation(out=r8b[:, :], in_=r8a[:, :],
                                                             func=AF.Exp, accum_out=s1b[:, :]),
                            "eS")
                kb.wait("vector", "eS", sev)
                kb.op_sync("vector", lambda e: e.reciprocal(s1c[:, :], s1b[:, :]))
                kb.op("vector", lambda e: e.tensor_scalar(r8b[:, :], r8b[:, :], s1c[:, 0:1],
                                                          None, ALU.mult))
                kb.op_sync("vector", lambda e: e.reduce_max(s1a[:, :], r8b[:, :], axis=AX.X))
                kb.op("vector", lambda e: e.tensor_scalar(r8a[:, :], r8b[:, :], s1a[:, 0:1],
                                                          None, ALU.subtract))
                kb.op("vector", lambda e: e.tensor_single_scalar(r8a[:, :], r8a[:, :], 0.0,
                                                                 ALU.is_ge))
                kb.op_sync("vector", lambda e: e.tensor_tensor_scan(r8b[:, :], ones8[:, :], r8a[:, :],
                                                               0.0, ALU.mult, ALU.add))
                kb.op("vector", lambda e: e.tensor_sub(r8b[:, :], r8b[:, :], r8a[:, :]))
                kb.op("vector", lambda e: e.tensor_single_scalar(r8b[:, :], r8b[:, :], 0.0,
                                                                 ALU.is_equal))
                kb.op("vector", lambda e, tile=tile: e.tensor_tensor(
                    oh_sb[:, tile * E:(tile + 1) * E], r8a[:, :], r8b[:, :], ALU.mult))
                kb.op("vector", lambda e, tile=tile: e.tensor_tensor(
                    r8a[:, :], oh_sb[:, tile * E:(tile + 1) * E], e07[:, :], ALU.mult))
                kb.op_sync("vector", lambda e, tile=tile: e.reduce_sum(
                    routes_sb[:, tile:tile + 1], r8a[:, :], axis=AX.X))
                ohv = kb.op("vector", lambda e, tile=tile: e.tensor_copy(
                    ohb_sb[:, :], oh_sb[:, tile * E:(tile + 1) * E]), "eV")
                kb.wait("tensor", "eV", ohv)
                kb.op("tensor", lambda e: e.matmul(pb[4][:, 8:16], tri[:, :], ohb_sb[:, :],
                                                   start=True, stop=True), "eP")
                kb.op("tensor", lambda e: e.matmul(pb[4][:, 16:24], onescf[0:1, :],
                                                   prefrow[0:1, :], start=True, stop=True), "eP")
                rkv = kb.op("tensor", lambda e, tile=tile: e.matmul(
                    pb[5][0:1, 0:E], ones128[:, 0:1], ohb_sb[:, :],
                    start=(tile == 0), stop=True, skip_group_check=True), "eP")
                kb.wait("vector", "eP", rkv)
                kb.op("vector", lambda e: e.tensor_copy(r8a[:, :], pb[4][:, 8:16]))
                kb.op("vector", lambda e: e.tensor_copy(r8b[:, :], pb[4][:, 16:24]))
                kb.op("vector", lambda e: e.tensor_add(r8a[:, :], r8a[:, :], r8b[:, :]))
                kb.op("vector", lambda e, tile=tile: e.tensor_tensor(
                    r8a[:, :], r8a[:, :], oh_sb[:, tile * E:(tile + 1) * E], ALU.mult))
                kb.op_sync("vector", lambda e, tile=tile: e.reduce_sum(
                    posl_sb[:, tile:tile + 1], r8a[:, :], axis=AX.X))
                epi_v = kb.op("vector", lambda e: e.tensor_copy(prefrow[0:1, :],
                                                                pb[5][0:1, 0:E]), "eV")
            sup_gate = epi_v
        seq_end_v = sup_gate
    ph1_done_v = seq_end_v
    ph1_done_p = kb.n("eP")

    if PHASES < 2:
        kb.wait("sync", "qS")
        kb.wait("sync", "qA")
        kb.wait("sync", "qG")
        kb.wait("sync", "qW")
        kb.wait("sync", "qI")
        kb.build()
        es.close()
        return nc
    kb.wait("vector", "eV", ph1_done_v)
    kb.wait("vector", "qW", 32)
    kb.op("vector", lambda e: e.tensor_copy(cnt8[0:1, :], prefrow[0:1, :]))
    c64v = kb.op("vector", lambda e: e.tensor_tensor(
        cnt64[0:1, :].rearrange("a (c e) -> a c e", c=8),
        cnt8[0:1, :].rearrange("a (z e) -> a z e", z=1).to_broadcast([1, 8, 8]),
        cmask_sb[0:1, :].rearrange("a (c e) -> a c e", c=8), ALU.mult), "eV")
    kb.wait("sync", "eV", c64v)
    kb.dma("sync", ccin_d[:, :], cnt64[0:1, :], "qS")
    kb.wait("gpsimd", "qS")
    ccv = kb.op("gpsimd", lambda e: e.collective_compute(
        "AllReduce", ALU.add, replica_groups=[list(range(8))],
        ins=[ccin_d], outs=[ccout_d]), "cc")
    kb.wait("sync", "cc", ccv)
    kb.dma("sync", cnt64[0:1, :], ccout_d[:, :], "qS")
    cntv = kb.n("qS")
    kb.wait("vector", "qS", cntv)
    kb.op("vector", lambda e: e.tensor_tensor(cnt64[0:1, :], cnt64[0:1, :], pmask_sb[0:1, :],
                                              ALU.mult))
    basev = kb.op_sync("vector", lambda e: e.reduce_sum(
        base8[0:1, :], cnt64[0:1, :].rearrange("a (c e) -> a e c", c=8), axis=AX.X), "eV")
    kb.wait("tensor", "eV", basev)
    kb.wait("tensor", "eP", ph1_done_p)
    bav = kb.op("tensor", lambda e: e.matmul(pb[4][:, 0:E], onescf[0:1, :], base8[0:1, :],
                                             start=True, stop=True), "eP")
    kb.wait("vector", "eP", bav)
    barv = kb.op("vector", lambda e: e.tensor_copy(basear[:, :], pb[4][:, 0:E]), "eV")
    kb.wait("vector", "eV", barv)
    kb.wait("gpsimd", "qA", sc_zeroed)
    xs_stored = kb.n("qS")
    kb.wait("gpsimd", "qS", xs_stored)
    slot_done = None
    disp_bufs = [xs_sb, xsT]
    idma_hist = [None, None]
    for tile in range(NT):
        kb.op("vector", lambda e, tile=tile: e.tensor_tensor(
            r8a[:, :], oh_sb[:, tile * E:(tile + 1) * E], basear[:, :], ALU.mult))
        kb.op_sync("vector", lambda e: e.reduce_sum(t2t[:, :], r8a[:, :], axis=AX.X))
        kb.op("vector", lambda e, tile=tile: e.tensor_tensor(
            t2t[:, :], t2t[:, :], posl_sb[:, tile:tile + 1], ALU.add))
        kb.op("vector", lambda e: e.tensor_single_scalar(keepf[:, :], t2t[:, :], -1.0, ALU.mult))
        kb.op("vector", lambda e: e.tensor_single_scalar(keepf[:, :], keepf[:, :], -4095.0,
                                                         ALU.is_ge))
        kb.op("vector", lambda e, tile=tile: e.tensor_scalar(
            t3t[:, :], routes_sb[:, tile:tile + 1], 4096.0, None, ALU.mult))
        kb.op("vector", lambda e: e.tensor_tensor(t3t[:, :], t3t[:, :], t2t[:, :], ALU.add))
        kb.op("vector", lambda e: e.tensor_scalar(slot_f[:, :], t3t[:, :], keepf[:, 0:1],
                                                  None, ALU.mult))
        kb.op("vector", lambda e, tile=tile: e.tensor_copy(gslot_i[:, tile:tile + 1],
                                                           slot_f[:, :]))
        kb.op("vector", lambda e: e.tensor_scalar(s1a[:, :], keepf[:, :], -32768.0, 32768.0,
                                                  ALU.mult, ALU.add))
        kb.op("vector", lambda e: e.tensor_tensor(slot_f[:, :], slot_f[:, :], s1a[:, :], ALU.add))
        slot_done = kb.op("vector", lambda e, tile=tile: e.tensor_copy(
            slot_i[:, tile:tile + 1], slot_f[:, :]), "eV")
        kb.wait("gpsimd", "eV", slot_done)
        dbuf = disp_bufs[tile % 2]
        if idma_hist[tile % 2] is not None:
            kb.wait("gpsimd", "qI", idma_hist[tile % 2])
        kb.dma("gpsimd", dbuf[:, :], xs_d[tile * P:(tile + 1) * P, :], "qG")
        kb.wait("gpsimd", "qG")
        kb.idma(sc_d[:, :], bass.IndirectOffsetOnAxis(slot_i[:, tile:tile + 1], 0),
                dbuf[:, :], None, "qI")
        idma_hist[tile % 2] = kb.n("qI")
    disp_v = kb.n("qI")
    if DBG:
        kb.wait("scalar", "eV", slot_done)
        kb.dma("scalar", dbg2_d[0], routes_sb[:, :], "qD4")
        kb.dma("scalar", dbg2_d[1], posl_sb[:, :], "qD5")
        kb.dma("scalar", dbg2_d[2, :, 0:E], basear[:, :], "qD6")
        kb.dma("scalar", dbg3_d[:, :], gslot_i[:, :], "qD7")
        kb.wait("vector", "qD7", 16)
    kb.wait("gpsimd", "qI", disp_v)
    rs_done = kb.op("gpsimd", lambda e: e.collective_compute(
        "ReduceScatter", ALU.add, replica_groups=[list(range(8))],
        ins=[sc_d[0:8 * CAP, :]], outs=[rsout_d]), "cc")

    if PHASES < 3:
        kb.wait("sync", "cc", rs_done)
        kb.wait("sync", "qS")
        kb.wait("sync", "qA")
        kb.wait("sync", "qG")
        kb.wait("sync", "qW")
        kb.wait("sync", "qI")
        kb.build()
        es.close()
        return nc
    kb.wait("gpsimd", "eV", ph1_done_v)
    kb.wait("gpsimd", "eP", ph1_done_p)
    for pch in range(32):
        dst = kh[:, pch * D:(pch + 1) * D] if pch < 17 else vt[:, (pch - 17) * D:(pch - 16) * D]
        kb.dma("gpsimd", dst, w2_in[pch * P:(pch + 1) * P, :], "qW")
    w12_ld = kb.n("qW")
    w1v = w1_in.rearrange("(dc p) (fs c) -> fs p dc c", p=P, c=P)
    def w1buf(fsub):
        half = fsub % 2
        return masks[:, half * 1024:(half + 1) * 1024]
    def w1tile(fsub, dc):
        half = fsub % 2
        return masks[:, half * 1024 + dc * P: half * 1024 + (dc + 1) * P]

    kb.wait("sync", "cc", rs_done)
    kb.wait("gpsimd", "cc", rs_done)
    sup_gate_p = None
    agin_stored = 0
    buf_read_done = None
    for su in range(8):
        if buf_read_done is not None:
            kb.wait("gpsimd", "eP", buf_read_done)
        for r4 in range(4):
            kb.dma("gpsimd", bufrow[:, r4 * D:(r4 + 1) * D],
                   rsout_d[su * 512 + r4 * P:su * 512 + (r4 + 1) * P, :], "qG")
        bl = kb.n("qG")
        kb.wait("tensor", "qG", bl)
        if su == 0:
            kb.wait("tensor", "qW", w12_ld)
        if sup_gate_p is not None:
            kb.wait("tensor", "eV", sup_gate_p)
        buf_tr = None
        for r4 in range(4):
            ptx = pt6 if r4 % 2 == 0 else pt7
            for ch in range(8):
                kb.op("tensor", lambda e, r4=r4, ch=ch, ptx=ptx: e.transpose(
                    ptx[:, ch * P:(ch + 1) * P],
                    bufrow[:, r4 * D + ch * P:r4 * D + (ch + 1) * P], idn[:, :]),
                    "eP" if ch == 7 else None)
            trv = kb.n("eP")
            if r4 == 3:
                buf_read_done = trv
            kb.wait("vector", "eP", trv)
            buf_tr = kb.op("vector", lambda e, r4=r4, ptx=ptx: e.tensor_copy(
                bufT[:, :].rearrange("p (c q) -> p c q", c=8)[:, :, r4 * P:(r4 + 1) * P],
                ptx[:, :].rearrange("p (c q) -> p c q", c=8)), "eV")
        kb.wait("tensor", "eV", buf_tr)
        if DBG and su == 0:
            kb.wait("gpsimd", "eV", buf_tr)
            kb.dma("gpsimd", dbg5_d[1, :, 0:8 * 512], bufT[:, :], "qDA")
            kb.wait("tensor", "qDA", 16)
        hbank_busy = [None, None]
        w1_ld = [None] * 33
        w1_used = [None, None]
        def load_w1(fsub):
            if w1_used[fsub % 2] is not None:
                kb.wait("gpsimd", "eP", w1_used[fsub % 2])
            kb.dma("gpsimd", w1buf(fsub), w1v[fsub], "qW")
            w1_ld[fsub] = kb.n("qW")
        load_w1(0)
        load_w1(1)
        ugel_last = None
        for fsub in range(32):
            bi = fsub % 2
            bank = pb[4 + bi]
            if hbank_busy[bi] is not None:
                kb.wait("tensor", "eS", hbank_busy[bi])
            kb.wait("tensor", "qW", w1_ld[fsub])
            for dc in range(8):
                kb.op("tensor", lambda e, dc=dc, fsub=fsub, bank=bank: e.matmul(
                    bank[:, :], w1tile(fsub, dc),
                    bufT[:, :].rearrange("p (c q) -> p c q", c=8)[:, dc, :],
                    start=(dc == 0), stop=(dc == 7)), "eP" if dc == 7 else None)
            hv = kb.n("eP")
            w1_used[bi] = hv
            if fsub + 2 < 32:
                load_w1(fsub + 2)
            kb.wait("scalar", "eP", hv)
            ugel_last = kb.op("scalar", lambda e, fsub=fsub, bank=bank: e.activation(
                out=uT[:, fsub * 512:(fsub + 1) * 512], in_=bank[:, :], func=AF.Gelu,
                bias=b1s[:, fsub:fsub + 1], scale=GELU_S), "eS")
            hbank_busy[bi] = ugel_last
        kb.wait("tensor", "eS", ugel_last)
        if DBG and su == 0:
            kb.wait("gpsimd", "eS", ugel_last)
            kb.dma("gpsimd", dbg5_d[0, :, 0:32 * 512], uT[:, :], "qDB")
            kb.wait("tensor", "qDB", 16)
            kb.wait("scalar", "qDB", 16)
        oT_v = None
        for dg in range(2):
            if oT_v is not None:
                kb.wait("tensor", "eV", oT_v)
            for f32i in range(32):
                pch = f32i
                for dcn4 in range(4):
                    dcn = dg * 4 + dcn4
                    kb.op("tensor", lambda e, pch=pch, dcn=dcn, dcn4=dcn4,
                          st=(pch == 0): e.matmul(
                              pb[dcn4][:, :], w2slice(pch, dcn),
                              uT[:, pch * 512:(pch + 1) * 512],
                              start=st, stop=True, skip_group_check=True),
                          "eP" if (pch == 31 and dcn4 == 3) else None)
            w2v = kb.n("eP")
            kb.wait("vector", "eP", w2v)
            for dcn4 in range(4):
                dcn = dg * 4 + dcn4
                kb.op("vector", lambda e, dcn4=dcn4: e.tensor_single_scalar(
                    sprime[:, :], pb[dcn4][:, :], 1.0 / GELU_S, ALU.mult))
                kb.op("vector", lambda e, dcn=dcn: e.tensor_scalar(
                    oTs(dcn), sprime[:, :], b2c[:, dcn:dcn + 1],
                    None, ALU.add), "eV" if dcn4 == 3 else None)
            oT_v = kb.n("eV")
        kb.wait("tensor", "eV", oT_v)
        if DBG and su == 0:
            kb.wait("gpsimd", "eV", oT_v)
            kb.dma("gpsimd", dbg5_d[1, :, 4096:4096 + 2048], sar[:, :], "qDC")
            kb.dma("gpsimd", dbg5_d[1, :, 4096 + 2048:4096 + 4096], ca1r[:, :], "qDD")
            kb.wait("tensor", "qDC", 16)
            kb.wait("tensor", "qDD", 16)
        sup_gate_p = None
        orow_hist = [None, None]
        for sub in range(4):
            ptx = pt6 if sub % 2 == 0 else pt7
            if orow_hist[sub % 2] is not None:
                kb.wait("tensor", "eV", orow_hist[sub % 2])
            for dcn in range(8):
                kb.op("tensor", lambda e, sub=sub, dcn=dcn, ptx=ptx: e.transpose(
                    ptx[:, dcn * P:(dcn + 1) * P],
                    oTs(dcn)[:, sub * P:(sub + 1) * P], idn[:, :]),
                    "eP" if dcn == 7 else None)
            trv = kb.n("eP")
            kb.wait("vector", "eP", trv)
            if sub > 0 or su > 0:
                kb.wait("vector", "qS", kb.n("qS"))
            orow_v = kb.op("vector", lambda e, ptx=ptx: e.tensor_copy(gst[:, :], ptx[:, :]),
                           "eV")
            kb.wait("sync", "eV", orow_v)
            kb.dma("sync", agin_d[su * 512 + sub * P:su * 512 + (sub + 1) * P, :], gst[:, :],
                   "qS")
            orow_hist[sub % 2] = orow_v
            sup_gate_p = orow_v
    agin_stored = kb.n("qS")
    kb.wait("gpsimd", "eP", kb.n("eP"))
    for b in range(BL):
        kb.dma("gpsimd", gbr[:, b * D:(b + 1) * D],
               misc_d[10 + b:11 + b, :].partition_broadcast(P), "qW")
    gbr_ld = kb.n("qW")
    kb.wait("vector", "qW", gbr_ld)
    gbr_v = g_transform(gbr)
    kb.wait("gpsimd", "qS", agin_stored)
    ag_done = kb.op("gpsimd", lambda e: e.collective_compute(
        "AllGather", ALU.bypass, replica_groups=[list(range(8))],
        ins=[agin_d], outs=[agout_d]), "cc")

    if PHASES < 4:
        kb.wait("sync", "cc", ag_done)
        kb.wait("sync", "qS")
        kb.wait("sync", "qA")
        kb.wait("sync", "qG")
        kb.wait("sync", "qW")
        kb.wait("sync", "qI")
        kb.build()
        es.close()
        return nc
    kb.wait("gpsimd", "cc", ag_done)
    kb.wait("sync", "cc", ag_done)
    if DBG or DBG_OUT:
        kb.wait("scalar", "cc", ag_done)
        kb.dma("scalar", rs_dump[:, :], rsout_d[:, :], "qA")
        kb.dma("scalar", ag_dump[:, :], agout_d[:, :], "qA")
        kb.dma("scalar", agi_dump[:, :], agin_d[:, :], "qA")
    fe_gst_free = [None, None]
    fe_xt0_free = None
    fe_store_prev = None
    g_bufs = [gst, sar[:, D:2 * D]]
    for tile in range(NT):
        b = tile // NTS
        tl = tile % NTS
        gbuf = g_bufs[tile % 2]
        if fe_gst_free[tile % 2] is not None:
            kb.wait("gpsimd", "eV", fe_gst_free[tile % 2])
        kb.idma(gbuf[:, :], None, agout_d[:, :],
                bass.IndirectOffsetOnAxis(gslot_i[:, tile:tile + 1], 0), "qI")
        gv = kb.n("qI")
        if fe_xt0_free is not None:
            kb.wait("sync", "eV", fe_xt0_free)
        kb.dma("sync", xt0[:, :], xnew_d[tile * P:(tile + 1) * P, :], "qS")
        xv = kb.n("qS")
        kb.wait("vector", "qS", xv)
        kb.wait("vector", "qI", gv)
        if fe_store_prev is not None:
            kb.wait("vector", "qS", fe_store_prev)
        kb.op("vector", lambda e, tile=tile: e.tensor_scalar(
            xsf_t[:, :], xt0[:, :], stats2[:, tile:tile + 1], None, ALU.subtract))
        kb.op("vector", lambda e, tile=tile: e.tensor_scalar(
            xsf_t[:, :], xsf_t[:, :], stats2[:, NT + tile:NT + tile + 1], None, ALU.mult))
        kb.op("vector", lambda e, b=b: e.tensor_tensor(
            xsf_t[:, :], xsf_t[:, :], cb1r[:, b * D:(b + 1) * D], ALU.mult))
        kb.op("vector", lambda e, b=b: e.tensor_tensor(
            xsf_t[:, :], xsf_t[:, :], sbr[:, b * D:(b + 1) * D], ALU.add))
        kb.op("vector", lambda e, tile=tile: e.tensor_tensor(
            r8a[:, :], oh_sb[:, tile * E:(tile + 1) * E], basear[:, :], ALU.mult))
        kb.op_sync("vector", lambda e: e.reduce_sum(t2t[:, :], r8a[:, :], axis=AX.X))
        kb.op("vector", lambda e, tile=tile: e.tensor_tensor(
            t2t[:, :], t2t[:, :], posl_sb[:, tile:tile + 1], ALU.add))
        kb.op("vector", lambda e: e.tensor_single_scalar(keepf[:, :], t2t[:, :], -1.0, ALU.mult))
        kb.op("vector", lambda e: e.tensor_single_scalar(keepf[:, :], keepf[:, :], -4095.0,
                                                         ALU.is_ge))
        gfree = kb.op("vector", lambda e, gbuf=gbuf: e.tensor_sub(
            dl1[:, :], gbuf[:, :], xsf_t[:, :]), "eV")
        fe_gst_free[tile % 2] = gfree
        kb.op("vector", lambda e: e.tensor_scalar(dl1[:, :], dl1[:, :], keepf[:, 0:1],
                                                  None, ALU.mult))
        kb.op("vector", lambda e: e.tensor_add(xsf_t[:, :], xsf_t[:, :], dl1[:, :]))
        kb.op("vector", lambda e, b=b: e.tensor_tensor(
            xsf_t[:, :], xsf_t[:, :], gbr[:, b * D:(b + 1) * D], ALU.mult))
        fv = kb.op("vector", lambda e: e.tensor_tensor(xt1[:, :], xsf_t[:, :], xt0[:, :],
                                                       ALU.add), "eV")
        fe_xt0_free = fv
        kb.wait("sync", "eV", fv)
        kb.dma("sync", out_d[b, tl * P:(tl + 1) * P, :], xt1[:, :], "qS")
        fe_store_prev = kb.n("qS")
    kb.wait("sync", "qS")
    kb.wait("sync", "qG")
    kb.wait("sync", "qW")
    kb.wait("sync", "qI")
    kb.wait("sync", "qA")

    kb.build()
    es.close()
    return nc


_NC_CACHE = [None]
_LAST_RES = [None]

def kernel(**inputs):
    x = np.ascontiguousarray(inputs["L0_attn_x"], dtype=np.float32)
    cvec = np.ascontiguousarray(inputs["L0_c"], dtype=np.float32)
    if _NC_CACHE[0] is None:
        _NC_CACHE[0] = build_program()
    nc = _NC_CACHE[0]
    in_maps = []
    for c in range(8):
        cm = np.zeros((1, 64), np.float32)
        cm[0, 8 * c:8 * c + 8] = 1.0
        pm = np.zeros((1, 64), np.float32)
        pm[0, :8 * c] = 1.0
        in_maps.append({
            "x": x[2 * c:2 * c + 2],
            "cvec": cvec[2 * c:2 * c + 2],
            "wmod": inputs["L0_wmod"],
            "gamma": inputs["L0_attn_gamma"],
            "wq": inputs["L0_attn_w_q"],
            "wkv": inputs["L0_attn_w_kv"],
            "nullkv": inputs["L0_attn_null_kv"],
            "qs": inputs["L0_attn_q_scale"],
            "ks": inputs["L0_attn_k_scale"],
            "wout": inputs["L0_attn_w_out"],
            "wswd": inputs["L0_ffn_W_sw"],
            "bsw": inputs["L0_ffn_b_sw"],
            "w1e": inputs["L0_ffn_W1"][c],
            "b1e": inputs["L0_ffn_b1"][c],
            "w2e": inputs["L0_ffn_W2"][c],
            "b2e": inputs["L0_ffn_b2"][c],
            "cmask": cm,
            "pmask": pm,
        })
    res = run_bass_kernel_spmd(nc, in_maps, core_ids=list(range(8)))
    if DBG or DBG_OUT:
        _LAST_RES[0] = res
    out = np.concatenate([np.asarray(r["out"]) for r in res.results], axis=0)
    return out.astype(np.float32)
